# version 01 (final)
# baseline (speedup 1.0000x reference)
import numpy as np
import ml_dtypes
from contextlib import ExitStack

import concourse.bass as bass
import concourse.mybir as mybir
from concourse.bass_utils import run_bass_kernel_spmd

F32 = mybir.dt.float32
BF16 = mybir.dt.bfloat16
I32 = mybir.dt.int32
AF = mybir.ActivationFunctionType
ALU = mybir.AluOpType
AX = mybir.AxisListType
BF = ml_dtypes.bfloat16

B, T, D = 2, 8192, 2048
H, KVH, DH = 16, 4, 128
MLP = 16384
NC = 8
TC = T // 4
CH = 512
NCH = TC // CH
HALO = 1024
WK = 12
SCALE_QK = 1.0 / np.sqrt(DH)
RSQRT4_DH = float(DH ** (-0.25))
L0_BIAS = -11.4
L0_BIAS_INVALID = -60.0

CAST_PARTS = [
    ("wq", D * D), ("wk", D * 512), ("wv", D * 512), ("wo", D * D),
    ("Wq", D * D), ("Wkvd", D * 256), ("Wkvu", 256 * 4096), ("Wo", D * D),
    ("lnb", D),
]
CAST_TILE = 128 * 4096
CAST_RAW = sum(n for _, n in CAST_PARTS)
CAST_N = ((CAST_RAW + NC * CAST_TILE - 1) // (NC * CAST_TILE)) * (NC * CAST_TILE)
SW_N = D * D



def _build_perms():
    JG = D // 2
    jp_ord = np.empty(JG, np.int64)
    for j in range(JG):
        jp_ord[(j % 16) * 64 + (j // 16)] = j
    pi = np.empty(D, np.int64)
    for c in range(8):
        for parity in range(2):
            q = 2 * c + parity
            for p in range(128):
                jp = 128 * c + (p // 64) * 64 + (p % 64)
                pi[128 * q + p] = 2 * jp_ord[jp] + parity
    JD = MLP // 2
    jpp_ord = np.empty(JD, np.int64)
    for j in range(JD):
        jpp_ord[(j % 16) * 512 + (j // 16)] = j
    sigma = np.empty(MLP, np.int64)
    for cpp in range(64):
        for parity in range(2):
            cc = 2 * cpp + parity
            for p in range(128):
                sigma[128 * cc + p] = 2 * jpp_ord[128 * cpp + p] + parity
    return pi, sigma, jp_ord, jpp_ord


PI, SIGMA, JP_ORD, JPP_ORD = _build_perms()

SQ_HEAD = np.concatenate([np.arange(0, DH, 2), np.arange(1, DH, 2)])


ENGS = ("sync", "vector", "tensor", "scalar", "gpsimd")


class Sched:

    def __init__(self, nc):
        self.nc = nc
        self.q = {e: [] for e in ENGS}
        self.nmark = {e: 0 for e in ENGS}
        self.ndma = {e: 0 for e in ("sync", "gpsimd", "scalar")}
        self.colls = []
        self.sem_prog = {}
        self.sem_dma = {}
        self.sem_coll = []

    def op(self, eng, fn):
        self.q[eng].append(["op", fn, None])

    def mark(self, eng):
        for it in reversed(self.q[eng]):
            if it[0] != "op":
                continue
            if it[2] is None:
                self.nmark[eng] += 1
                it[2] = ("prog", eng, self.nmark[eng])
                return it[2]
            if it[2][0] == "prog":
                return it[2]
            raise AssertionError("cannot prog-mark an op with dma/coll inc")
        raise AssertionError("no op to mark")

    def dma(self, eng, out, in_, **kw):
        def fn(nc=self.nc, out=out, in_=in_, kw=kw):
            return getattr(nc, eng).dma_start(out=out, in_=in_, **kw)
        self.q[eng].append(["op", fn, ("dma", eng)])
        self.ndma[eng] += 1
        return ("dma", eng, 16 * self.ndma[eng])

    def collective(self, ins, outs, kind="AllGather", op=ALU.bypass):
        idx = len(self.colls)
        self.colls.append(None)

        def fn(nc=self.nc, ins=ins, outs=outs, kind=kind, op=op):
            return nc.gpsimd.collective_compute(
                kind, op, replica_groups=[list(range(NC))], ins=ins, outs=outs)
        self.q["gpsimd"].append(["op", fn, ("coll", idx)])
        return ("coll", idx, 1)

    def wait(self, eng, tok):
        self.q[eng].append(["wait", tok, None])

    def fence(self, eng):
        tm = self.mark(eng)
        self.wait(eng, tm)
        return tm

    def waits(self, eng, toks):
        for t in toks:
            self.wait(eng, t)

    def build_block(self, stack):
        nc = self.nc
        for e in ENGS:
            self.sem_prog[e] = stack.enter_context(nc.semaphore(f"sp_{e}"))
        for e in ("sync", "gpsimd", "scalar"):
            self.sem_dma[e] = stack.enter_context(nc.semaphore(f"sd_{e}"))
        for i in range(len(self.colls)):
            self.sem_coll.append(stack.enter_context(nc.semaphore(f"sc_{i}")))
        block = stack.enter_context(nc.Block())

        def sem_of(tok):
            kind = tok[0]
            if kind == "prog":
                return self.sem_prog[tok[1]], tok[2]
            if kind == "dma":
                return self.sem_dma[tok[1]], tok[2]
            if kind == "coll":
                return self.sem_coll[tok[1]], 1
            raise ValueError(tok)

        def replay(eng):
            def body(E):
                for item in self.q[eng]:
                    if item[0] == "wait":
                        sem, val = sem_of(item[1])
                        E.wait_ge(sem, val)
                    else:
                        inst = item[1]()
                        inc = item[2]
                        if inc is not None:
                            if inc[0] == "dma":
                                inst.then_inc(self.sem_dma[inc[1]], 16)
                            elif inc[0] == "prog":
                                inst.then_inc(self.sem_prog[inc[1]], 1)
                            elif inc[0] == "coll":
                                inst.then_inc(self.sem_coll[inc[1]])
            return body

        block.sync(replay("sync"))
        block.vector(replay("vector"))
        block.tensor(replay("tensor"))
        block.scalar(replay("scalar"))
        block.gpsimd(replay("gpsimd"))



def e_ts(s, eng, out, in0, s1, s2=None, op0=ALU.mult, op1=ALU.add):
    def fn():
        kw = {} if s2 is None else {"op1": op1}
        return getattr(s.nc, eng).tensor_scalar(
            out=out, in0=in0, scalar1=s1, scalar2=s2, op0=op0, **kw)
    s.op(eng, fn)


def e_tt(s, eng, out, in0, in1, op=ALU.mult):
    s.op(eng, lambda: getattr(s.nc, eng).tensor_tensor(out=out, in0=in0, in1=in1, op=op))


def e_stt(s, eng, out, in0, scalar, in1, op0, op1):
    s.op(eng, lambda: getattr(s.nc, eng).scalar_tensor_tensor(
        out=out, in0=in0, scalar=scalar, in1=in1, op0=op0, op1=op1))


def e_copy(s, eng, out, in_):
    s.op(eng, lambda: getattr(s.nc, eng).tensor_copy(out, in_))


def e_memset(s, eng, ap, v):
    s.op(eng, lambda: getattr(s.nc, eng).memset(ap, v))


def e_act(s, out, in_, func, bias=0, scale=1, accum_out=None):
    def fn():
        kw = {} if accum_out is None else {"accum_out": accum_out}
        return s.nc.scalar.activation(out, in_, func, bias=bias, scale=scale, **kw)
    s.op("scalar", fn)


def e_mm(s, out, lhsT, rhs, start, stop):
    s.op("tensor", lambda: s.nc.tensor.matmul(out, lhsT, rhs, start=start, stop=stop))


def e_tr(s, out, in_, ident):
    s.op("tensor", lambda: s.nc.tensor.transpose(out, in_, ident))


def e_red(s, out, in_, op=ALU.max, absval=False):
    s.op("vector", lambda: s.nc.vector.tensor_reduce(
        out, in_, axis=AX.X, op=op,
        **({"apply_absolute_value": True} if absval else {})))


def e_recip(s, out, in_):
    s.op("vector", lambda: s.nc.vector.reciprocal(out, in_))



def build_z():
    nc = bass.Bass("TRN2", target_bir_lowering=False)
    s = Sched(nc)

    pg = nc.dram_tensor("pg", [128, MLP], I32, kind="ExternalInput").ap()
    pu = nc.dram_tensor("pu", [128, MLP], I32, kind="ExternalInput").ap()
    pd = nc.dram_tensor("pd", [1024, D], I32, kind="ExternalInput").ap()
    scg = nc.dram_tensor("scg", [64, MLP], F32, kind="ExternalInput").ap()
    scu = nc.dram_tensor("scu", [64, MLP], F32, kind="ExternalInput").ap()
    scd = nc.dram_tensor("scd", [512, D], F32, kind="ExternalInput").ap()
    wflat = nc.dram_tensor("wflat", [CAST_N // NC], F32, kind="ExternalInput").ap()
    swsl = nc.dram_tensor("swsl", [SW_N // NC], F32, kind="ExternalInput").ap()
    identf_in = nc.dram_tensor("identf_in", [128, 128], F32, kind="ExternalInput").ap()

    dg = nc.dram_tensor("dg", [2, 128, MLP], BF16, kind="ExternalOutput").ap()
    du = nc.dram_tensor("du", [2, 128, MLP], BF16, kind="ExternalOutput").ap()
    dd = nc.dram_tensor("dd", [16, 128, D], BF16, kind="ExternalOutput").ap()
    wb = nc.dram_tensor("wb", [CAST_N // NC], BF16, kind="ExternalOutput").ap()
    swb = nc.dram_tensor("swb", [SW_N // NC], BF16, kind="ExternalOutput").ap()
    wmax = nc.dram_tensor("wmax", [1, 1], F32, kind="ExternalOutput").ap()

    with ExitStack() as stack:
        sb = lambda n_, sh, d: stack.enter_context(nc.sbuf_tensor(n_, sh, d))
        xi = [sb(f"xi{i}", [128, 2048], I32) for i in range(2)]
        t_ = sb("t_", [128, 2048], I32)
        a_ = sb("a_", [128, 2048], I32)
        b_ = sb("b_", [128, 2048], I32)
        bits = sb("bits", [128, 2048], I32)
        stile = [sb(f"stile{i}", [128, 2048], F32) for i in range(2)]
        dec = [sb(f"dec{i}", [128, 2048], BF16) for i in range(2)]
        wtile = [sb(f"wtile{i}", [128, 4096], F32) for i in range(2)]
        wcast = [sb(f"wcast{i}", [128, 4096], BF16) for i in range(2)]
        swc = [sb(f"swc{i}", [128, 4096], BF16) for i in range(2)]
        amx = sb("amx", [128, 1], F32)
        amx2 = sb("amx2", [128, 1], F32)
        identf = sb("identf", [128, 128], F32)
        rmax = sb("rmax", [1, 128], F32)
        rmax2 = sb("rmax2", [1, 1], F32)
        prow = stack.enter_context(nc.psum_tensor("prow", [1, 128], F32))

        t_id = s.dma("sync", identf[:], identf_in)

        def decode_plane(xi_t, lo, out_bits):
            if lo:
                e_ts(s, "vector", t_[:], xi_t, 0x7, None, ALU.bitwise_and)
            else:
                e_ts(s, "vector", t_[:], xi_t, 4, 0x7,
                     ALU.logical_shift_right, ALU.bitwise_and)
            e_ts(s, "vector", a_[:], t_[:], 6, None, ALU.logical_shift_left)
            e_ts(s, "vector", a_[:], a_[:], 0x3F00, None, ALU.add)
            e_ts(s, "vector", b_[:], t_[:], 0x3F00, None, ALU.mult)
            e_tt(s, "vector", b_[:], a_[:], b_[:], ALU.subtract)
            e_ts(s, "vector", t_[:], t_[:], 2, None, ALU.is_ge)
            e_ts(s, "vector", t_[:], t_[:], -1, None, ALU.add)
            e_tt(s, "vector", b_[:], b_[:], t_[:], ALU.mult)
            e_tt(s, "vector", a_[:], a_[:], b_[:], ALU.add)
            e_ts(s, "vector", b_[:], xi_t, (12 if lo else 8), 0x8000,
                 ALU.logical_shift_left, ALU.bitwise_and)
            e_tt(s, "vector", out_bits[:], a_[:], b_[:], ALU.bitwise_or)

        buf_tok = [None, None]
        dec_tok = [None, None]
        dec_st = {"di": 0, "it": 0}

        def decode_matrix(psrc, dout, nrow_chunks, ncol, scale_src):
            nsl = ncol // 2048
            for rc in range(nrow_chunks):
                for sl in range(nsl):
                    bi = dec_st["it"] % 2
                    dec_st["it"] += 1
                    if buf_tok[bi] is not None:
                        s.wait("sync", buf_tok[bi])
                    tok = s.dma("sync", xi[bi][:],
                                psrc[rc * 128:(rc + 1) * 128, sl * 2048:(sl + 1) * 2048])
                    s0 = s.dma("sync", stile[bi][0:64, :], scale_src(rc, sl, 0))
                    s1 = s.dma("sync", stile[bi][64:128, :], scale_src(rc, sl, 1))
                    s.waits("vector", [tok, s0, s1])
                    for parity in (0, 1):
                        di = dec_st["di"]
                        decode_plane(xi[bi][:], parity == 0, bits)
                        if dec_tok[di] is not None:
                            s.wait("vector", dec_tok[di])
                        bfv = bits[:].bitcast(BF16)
                        e_tt(s, "vector", dec[di][:],
                             bfv[:, 0:2 * 2048:2], stile[bi][:], ALU.mult)
                        tm = s.mark("vector")
                        if parity == 1:
                            buf_tok[bi] = tm
                        s.wait("gpsimd", tm)
                        dec_tok[di] = s.dma("gpsimd", dout(rc, parity, sl), dec[di][:])
                        dec_st["di"] ^= 1

        def sc_gate(scsrc):
            def f(rc, sl, half):
                return scsrc[0:64, sl * 2048:(sl + 1) * 2048]
            return f

        def sc_down(rc, sl, half):
            r0 = 128 * (rc % 4) + 64 * half
            return scd[r0:r0 + 64, sl * 2048:(sl + 1) * 2048]

        decode_matrix(pg, lambda rc, par, sl: dg[par, :, sl * 2048:(sl + 1) * 2048],
                      1, MLP, sc_gate(scg))
        decode_matrix(pu, lambda rc, par, sl: du[par, :, sl * 2048:(sl + 1) * 2048],
                      1, MLP, sc_gate(scu))
        decode_matrix(pd, lambda rc, par, sl: dd[2 * rc + par, :, sl * 2048:(sl + 1) * 2048],
                      8, D, sc_down)

        ncast_t = (CAST_N // NC) // CAST_TILE
        wflat2 = wflat.rearrange("(a p n) -> a p n", p=128, n=4096)
        wb2 = wb.rearrange("(a p n) -> a p n", p=128, n=4096)
        in_tok = [None, None]
        out_tok = [None, None]
        for i in range(ncast_t):
            bi = i % 2
            if in_tok[bi] is not None:
                s.wait("sync", in_tok[bi])
            tok = s.dma("sync", wtile[bi][:], wflat2[i])
            s.wait("vector", tok)
            if out_tok[bi] is not None:
                s.wait("vector", out_tok[bi])
            e_copy(s, "vector", wcast[bi][:], wtile[bi][:])
            in_tok[bi] = tm = s.mark("vector")
            s.wait("gpsimd", tm)
            out_tok[bi] = s.dma("gpsimd", wb2[i], wcast[bi][:])

        nsw_t = (SW_N // NC) // CAST_TILE
        sw2 = swsl.rearrange("(a p n) -> a p n", p=128, n=4096)
        swb2 = swb.rearrange("(a p n) -> a p n", p=128, n=4096)
        e_memset(s, "vector", amx[:], 0.0)
        tokv = None
        for i in range(nsw_t):
            bi = i % 2
            if in_tok[bi] is not None:
                s.wait("sync", in_tok[bi])
                in_tok[bi] = None
            tok = s.dma("sync", wtile[bi][:], sw2[i])
            s.wait("scalar", tok)
            e_act(s, swc[bi][:], wtile[bi][:], AF.Sign)
            tm = s.mark("scalar")
            s.wait("gpsimd", tm)
            s.dma("gpsimd", swb2[i], swc[bi][:])
            s.wait("vector", tok)
            e_red(s, amx2[:], wtile[bi][:], ALU.max, absval=True)
            s.fence("vector")
            e_tt(s, "vector", amx[:], amx[:], amx2[:], ALU.max)
            tokv = s.mark("vector")
        s.wait("tensor", tokv)
        s.wait("tensor", t_id)
        e_tr(s, prow[:, :], amx[:, :], identf[:])
        tokt = s.mark("tensor")
        s.wait("vector", tokt)
        e_copy(s, "vector", rmax[:], prow[:, :])
        e_red(s, rmax2[:], rmax[:], ALU.max)
        tokv2 = s.mark("vector")
        s.wait("gpsimd", tokv2)
        tout = s.dma("gpsimd", wmax, rmax2[:])
        s.wait("gpsimd", tout)

        s.build_block(stack)
    return nc



def _pack_T_gate(Wp, sc):
    Wp_s = Wp[SIGMA]
    sc_s = sc[SIGMA]
    packedT = np.ascontiguousarray(Wp_s.T[JP_ORD])
    scT = np.ascontiguousarray(sc_s.T)
    return packedT, scT


def _pack_T_down(Wp, sc):
    packedT = np.ascontiguousarray(Wp.T[JPP_ORD])
    scT = np.ascontiguousarray(sc.T)
    return packedT, scT


def _z_inputs(inputs):
    pgT, scgT = _pack_T_gate(np.asarray(inputs["L0_ffn_gate_W_packed"]),
                             np.asarray(inputs["L0_ffn_gate_scales"]))
    puT, scuT = _pack_T_gate(np.asarray(inputs["L0_ffn_up_W_packed"]),
                             np.asarray(inputs["L0_ffn_up_scales"]))
    pdT, scdT = _pack_T_down(np.asarray(inputs["L0_ffn_down_W_packed"]),
                             np.asarray(inputs["L0_ffn_down_scales"]))

    wq = np.asarray(inputs["L0_attn_wq"])
    wk = np.asarray(inputs["L0_attn_wk"])
    wv = np.asarray(inputs["L0_attn_wv"])
    wo = np.asarray(inputs["L0_attn_wo"])
    sq_full = (np.arange(H)[:, None] * DH + SQ_HEAD[None, :]).ravel()
    sk_full = (np.arange(KVH)[:, None] * DH + SQ_HEAD[None, :]).ravel()
    wq_d = np.ascontiguousarray(wq.T[PI][:, sq_full])
    wk_d = np.ascontiguousarray(wk.T[PI][:, sk_full])
    wv_d = np.ascontiguousarray(wv.T[PI])
    wo_d = np.ascontiguousarray(wo.T)
    Wq_d = np.ascontiguousarray(np.asarray(inputs["L1_attn_Wq"])[PI])
    Wkvd_d = np.ascontiguousarray(np.asarray(inputs["L1_attn_Wkv_down"])[PI])
    Wkvu_d = np.ascontiguousarray(np.asarray(inputs["L1_attn_Wkv_up"]))
    Wo_d = np.ascontiguousarray(np.asarray(inputs["L1_attn_Wo"]))
    lnb_pi = np.ascontiguousarray(np.asarray(inputs["L1_ffn_ln_bias"])[PI])

    parts = [wq_d, wk_d, wv_d, wo_d, Wq_d, Wkvd_d, Wkvu_d, Wo_d, lnb_pi]
    flat = np.concatenate([p.ravel() for p in parts]).astype(np.float32)
    flat = np.concatenate([flat, np.zeros(CAST_N - len(flat), np.float32)])

    swT = np.ascontiguousarray(np.asarray(inputs["L1_ffn_weight"]).T[PI])
    swflat = swT.ravel().astype(np.float32)

    identf = np.eye(128, dtype=np.float32)
    in_maps = []
    for c in range(NC):
        in_maps.append(dict(
            pg=pgT[128 * c:128 * (c + 1)].astype(np.int32),
            pu=puT[128 * c:128 * (c + 1)].astype(np.int32),
            pd=pdT[1024 * c:1024 * (c + 1)].astype(np.int32),
            scg=scgT, scu=scuT, scd=scdT,
            wflat=flat[c * (CAST_N // NC):(c + 1) * (CAST_N // NC)],
            swsl=swflat[c * (SW_N // NC):(c + 1) * (SW_N // NC)],
            identf_in=identf,
        ))
    shapes = [p.shape for p in parts]
    return in_maps, shapes


def _z_post(results, shapes):
    decG = np.concatenate([r["dg"] for r in results], axis=1)
    dG = np.empty((D, MLP), BF)
    dU = np.empty((D, MLP), BF)
    for c in range(NC):
        for par in range(2):
            dG[128 * (2 * c + par):128 * (2 * c + par + 1)] = results[c]["dg"][par]
            dU[128 * (2 * c + par):128 * (2 * c + par + 1)] = results[c]["du"][par]
    dD = np.empty((MLP, D), BF)
    for c in range(NC):
        dD[2048 * c:2048 * (c + 1)] = results[c]["dd"].reshape(2048, D)
    wcat = np.concatenate([r["wb"] for r in results])
    out = {}
    off = 0
    for (name, n), shp in zip(CAST_PARTS, shapes):
        out[name] = wcat[off:off + n].reshape(shp)
        off += n
    out["signW"] = np.concatenate([r["swb"] for r in results]).reshape(D, D)
    out["wmax8"] = np.stack([r["wmax"].reshape(()) for r in results]).reshape(1, 8)
    out["dG"], out["dU"], out["dD"] = dG, dU, dD
    return out



GELU_A = 2 * 0.7978845608028654
GELU_B = 2 * 0.7978845608028654 * 0.044715


def build_a():
    nc = bass.Bass("TRN2", target_bir_lowering=False)
    s = Sched(nc)

    xa = nc.dram_tensor("xa", [HALO + TC, D], F32, kind="ExternalInput").ap()
    wq_t = nc.dram_tensor("wq_t", [H, D, DH], BF16, kind="ExternalInput").ap()
    wk_t = nc.dram_tensor("wk_t", [KVH, D, DH], BF16, kind="ExternalInput").ap()
    wv_t = nc.dram_tensor("wv_t", [16, 128, 512], BF16, kind="ExternalInput").ap()
    wo_t = nc.dram_tensor("wo_t", [16, D, 128], BF16, kind="ExternalInput").ap()
    dg_t = nc.dram_tensor("dg_t", [128, 16, 128, 128], BF16, kind="ExternalInput").ap()
    du_t = nc.dram_tensor("du_t", [128, 16, 128, 128], BF16, kind="ExternalInput").ap()
    dd_t = nc.dram_tensor("dd_t", [16, 128, 128, 128], BF16, kind="ExternalInput").ap()
    cos_in = nc.dram_tensor("cos_in", [64, HALO + TC], F32, kind="ExternalInput").ap()
    sin_in = nc.dram_tensor("sin_in", [64, HALO + TC], F32, kind="ExternalInput").ap()
    masks_in = nc.dram_tensor("masks_in", [128, 4 * CH], BF16, kind="ExternalInput").ap()
    biasT_in = nc.dram_tensor("biasT_in", [128, NCH * WK], F32, kind="ExternalInput").ap()
    onesc_in = nc.dram_tensor("onesc_in", [128, 1], BF16, kind="ExternalInput").ap()
    onesf_in = nc.dram_tensor("onesf_in", [128, 1], F32, kind="ExternalInput").ap()
    onesr_in = nc.dram_tensor("onesr_in", [1, 128], F32, kind="ExternalInput").ap()
    c297_in = nc.dram_tensor("c297_in", [1, 128], F32, kind="ExternalInput").ap()
    identb_in = nc.dram_tensor("identb_in", [128, 128], BF16, kind="ExternalInput").ap()
    identf_in = nc.dram_tensor("identf_in", [128, 128], F32, kind="ExternalInput").ap()
    cbias_in = nc.dram_tensor("cbias_in", [128, 2], F32, kind="ExternalInput").ap()

    x1 = nc.dram_tensor("x1", [TC, D], F32, kind="ExternalOutput").ap()

    with ExitStack() as stack:
        sb = lambda n_, sh, d: stack.enter_context(nc.sbuf_tensor(n_, sh, d))
        ps = lambda n_, sh, d: stack.enter_context(nc.psum_tensor(n_, sh, d))

        cosb = sb("cosb", [64, CH], F32)
        sinb = sb("sinb", [64, CH], F32)
        masks = sb("masks", [128, 4 * CH], BF16)
        biasT = sb("biasT", [128, NCH * WK], F32)
        onesc = sb("onesc", [128, 1], BF16)
        onesf = sb("onesf", [128, 1], F32)
        onesr = sb("onesr", [1, 128], F32)
        c297 = sb("c297", [1, 128], F32)
        identb = sb("identb", [128, 128], BF16)
        identf = sb("identf", [128, 128], F32)
        cbias = sb("cbias", [128, 2], F32)
        kr = [sb(f"kr{i}", [128, KVH * CH], BF16) for i in range(3)]
        vr = [sb(f"vr{i}", [128, KVH * CH], BF16) for i in range(3)]

        R1 = [sb(f"R1_{i}", [128, 2048], F32) for i in range(2)]
        xnb = [sb(f"xnb{i}", [128, D], BF16) for i in range(4)]
        xnfm = sb("xnfm", [128, 16 * CH], BF16)
        qhat = sb("qhat", [128, H * CH], BF16)
        yhat = sb("yhat", [128, H * CH], BF16)
        Pt = [sb(f"P{i}", [128, CH], BF16) for i in range(2)]
        wsl = [sb(f"wsl{i}", [128, 2048], BF16) for i in range(2)]
        wup = [sb(f"wup{i}", [128, 2048], BF16) for i in range(2)]
        wdn = [sb(f"wdn{i}", [128, 2048], BF16) for i in range(2)]
        dacc = sb("dacc", [128, 16 * CH], F32)
        qsq = sb("qsq", [128, CH], F32)
        bcb = sb("bcb", [128, CH], F32)
        ro = sb("ro", [128, 2 * CH], F32)
        gbuf = sb("gbuf", [128, CH], F32)
        g2buf = sb("g2buf", [128, CH], F32)
        wbuf = sb("wbuf", [128, CH], F32)
        sgb = sb("sgb", [128, CH], F32)
        preb = sb("preb", [128, CH], F32)
        srow = sb("srow", [1, CH], F32)
        cols = sb("cols", [128, 8], F32)

        junk_ap = dacc[:, 0:2048]
        x1buf = qhat[:].bitcast(F32)[:, 0:2048]

        def it_ap(i):
            return R1[i // 8][:].bitcast(BF16)[:, CH * (i % 8):CH * (i % 8 + 1)]

        p_proj = ps("p_proj", [128, CH], F32)
        p_sc0 = ps("p_sc0", [128, CH], F32)
        p_sc1 = ps("p_sc1", [128, CH], F32)
        p_y = ps("p_y", [128, CH], F32)
        p_row = ps("p_row", [1, CH], F32)
        p_bc = ps("p_bc", [128, CH], F32)
        p_trb = ps("p_trb", [128, 128], BF16)
        p_trf = ps("p_trf", [128, 128], F32)

        pre_toks = [
            s.dma("sync", masks[:], masks_in),
            s.dma("sync", biasT[:], biasT_in),
            s.dma("sync", onesc[:], onesc_in),
            s.dma("sync", onesf[:], onesf_in),
            s.dma("sync", onesr[:], onesr_in),
            s.dma("sync", c297[:], c297_in),
            s.dma("sync", identb[:], identb_in),
            s.dma("sync", identf[:], identf_in),
            s.dma("sync", cbias[:], cbias_in),
        ]
        s.waits("vector", pre_toks)
        s.waits("tensor", pre_toks)
        s.waits("scalar", [pre_toks[1]])

        guard = {}

        def g_wait(eng, key):
            t = guard.get(key)
            if t is not None:
                s.wait(eng, t)
                guard[key] = None

        def emit_ln_chunk(row0):
            g_wait("sync", "cosbuf")
            tc_ = s.dma("sync", cosb[:], cos_in[:, row0:row0 + CH])
            ts_ = s.dma("sync", sinb[:], sin_in[:, row0:row0 + CH])
            s.waits("vector", [tc_, ts_])
            for ts4 in range(4):
                bi = ts4 % 2
                g_wait("sync", ("R1", bi))
                tok = s.dma("sync", R1[bi][:], xa[row0 + 128 * ts4:row0 + 128 * (ts4 + 1), :])
                s.wait("vector", tok)
                s.wait("scalar", tok)
                xt = R1[bi]
                s.op("vector", lambda xt=xt: nc.vector.reduce_sum(
                    cols[:, 0:1], xt[:], axis=AX.X))
                s.fence("vector")
                g_wait("scalar", "junk")
                e_act(s, junk_ap, xt[:], AF.Square, accum_out=cols[:, 1:2])
                guard["junk"] = tka = s.mark("scalar")
                e_ts(s, "vector", cols[:, 2:3], cols[:, 0:1], 1.0 / D, None, ALU.mult)
                s.fence("vector")
                e_tt(s, "vector", cols[:, 3:4], cols[:, 2:3], cols[:, 2:3], ALU.mult)
                s.wait("vector", tka)
                s.fence("vector")
                e_stt(s, "vector", cols[:, 4:5], cols[:, 1:2], 1.0 / D, cols[:, 3:4],
                      ALU.mult, ALU.subtract)
                tkv = s.mark("vector")
                s.wait("scalar", tkv)
                e_act(s, cols[:, 5:6], cols[:, 4:5], AF.Sqrt, bias=cbias[:, 0:1])
                tks = s.mark("scalar")
                s.wait("vector", tks)
                e_recip(s, cols[:, 6:7], cols[:, 5:6])
                s.fence("vector")
                g_wait("vector", ("xnb", ts4))
                for rho in range(2):
                    for par in range(2):
                        out_ap = bass.AP(xnb[ts4][:].tensor, 128 * par + 64 * rho,
                                         [[2048, 128], [1, 64], [256, 8]])
                        in_ap = bass.AP(xt[:].tensor, 2 * rho + par,
                                        [[2048, 128], [32, 64], [4, 8]])

                        def fn(out_ap=out_ap, in_ap=in_ap):
                            return nc.vector.tensor_scalar(
                                out=out_ap, in0=in_ap, scalar1=cols[:, 2:3],
                                scalar2=cols[:, 6:7], op0=ALU.subtract, op1=ALU.mult)
                        s.op("vector", fn)
                guard[("R1", bi)] = tfin = s.mark("vector")
                s.wait("tensor", tfin)
            g_wait("vector", "xnfm")
            g_wait("tensor", "ptrb")
            tr_tok = None
            for kq in range(16):
                for ts4 in range(4):
                    if tr_tok is not None:
                        s.wait("tensor", tr_tok)
                    e_tr(s, p_trb[:, :], xnb[ts4][:, 128 * kq:128 * (kq + 1)], identb[:])
                    tk = s.mark("tensor")
                    s.wait("vector", tk)
                    e_copy(s, "vector",
                           xnfm[:, CH * kq + 128 * ts4:CH * kq + 128 * (ts4 + 1)],
                           p_trb[:, :])
                    tr_tok = s.mark("vector")
            for i in range(4):
                guard[("xnb", i)] = tr_tok
            guard["ptrb"] = tr_tok
            s.wait("tensor", tr_tok)

        def emit_rope_head(dst, off, slab):
            g_wait("tensor", "pproj")
            for kq in range(16):
                e_mm(s, p_proj[:, :], slab[:, 128 * kq:128 * (kq + 1)],
                     xnfm[:, CH * kq:CH * kq + CH], start=(kq == 0), stop=(kq == 15))
            tmm = s.mark("tensor")
            s.wait("scalar", tmm)
            g_wait("scalar", "qsq")
            e_act(s, qsq[:], p_proj[:, :], AF.Square)
            tsq = s.mark("scalar")
            g_wait("tensor", "prow")
            s.wait("tensor", tsq)
            e_mm(s, p_row[:, :], onesf[:, :], qsq[:], start=True, stop=True)
            guard["qsq"] = trow = s.mark("tensor")
            s.wait("scalar", trow)
            e_act(s, srow[:], p_row[:, :], AF.Sqrt, bias=cbias[0:1, 1:2], scale=1.0 / DH)
            guard["prow"] = tsr = s.mark("scalar")
            s.wait("vector", tsr)
            g_wait("vector", "srow")
            e_recip(s, srow[:], srow[:])
            trr = s.fence("vector")
            g_wait("tensor", "pbc")
            s.wait("tensor", trr)
            e_mm(s, p_bc[:, :], c297[:, :], srow[:], start=True, stop=True)
            guard["srow"] = tbc = s.mark("tensor")
            s.wait("vector", tbc)
            e_copy(s, "vector", bcb[:], p_bc[:, :])
            e_tt(s, "vector", ro[0:64, 0:CH], p_proj[0:64, :], cosb[:], ALU.mult)
            e_tt(s, "vector", ro[0:64, CH:2 * CH], p_proj[64:128, :], sinb[:], ALU.mult)
            e_tt(s, "vector", ro[64:128, 0:CH], p_proj[0:64, :], sinb[:], ALU.mult)
            e_tt(s, "vector", ro[64:128, CH:2 * CH], p_proj[64:128, :], cosb[:], ALU.mult)
            e_tt(s, "vector", ro[0:64, 0:CH], ro[0:64, 0:CH], ro[0:64, CH:2 * CH],
                 ALU.subtract)
            e_tt(s, "vector", ro[64:128, 0:CH], ro[64:128, 0:CH],
                 ro[64:128, CH:2 * CH], ALU.add)
            e_tt(s, "vector", dst[0:64, off:off + CH], ro[0:64, 0:CH],
                 bcb[0:64, :], ALU.mult)
            e_tt(s, "vector", dst[64:128, off:off + CH], ro[64:128, 0:CH],
                 bcb[64:128, :], ALU.mult)
            tdone = s.mark("vector")
            guard["pproj"] = tdone
            guard["pbc"] = tdone

        def emit_kv_chunk(slot):
            banks = [p_proj, p_sc0, p_sc1, p_y]
            g_wait("tensor", "pproj")
            g_wait("tensor", "psc0")
            g_wait("tensor", "psc1")
            g_wait("tensor", "py")
            g_wait("vector", ("vr", slot))
            for kq in range(16):
                bi = kq % 2
                g_wait("sync", ("wsl", bi))
                tok = s.dma("sync", wsl[bi][:, 0:512], wv_t[kq])
                s.wait("tensor", tok)
                for ts4 in range(4):
                    e_mm(s, banks[ts4][:, :],
                         xnfm[:, CH * kq + 128 * ts4:CH * kq + 128 * (ts4 + 1)],
                         wsl[bi][:, 0:512], start=(kq == 0), stop=(kq == 15))
                guard[("wsl", bi)] = s.mark("tensor")
            tmm = guard[("wsl", 1)]
            s.wait("vector", tmm)
            for ts4 in range(4):
                e_copy(s, "vector", vr[slot][:, CH * ts4:CH * (ts4 + 1)], banks[ts4][:, :])
            tve = s.mark("vector")
            s.wait("tensor", tve)
            for kv in range(KVH):
                bi = kv % 2
                g_wait("sync", ("wsl", bi))
                tok = s.dma("sync", wsl[bi][:].rearrange("p (kq j) -> p kq j", kq=16),
                            wk_t[kv].rearrange("(kq p) j -> p kq j", p=128))
                s.wait("tensor", tok)
                emit_rope_head(kr[slot], CH * kv, wsl[bi])
                guard[("wsl", bi)] = guard["pproj"]
            guard["xnfm"] = guard["pproj"]
            guard["cosbuf"] = guard["pproj"]

        def emit_attn_chunk(c):
            g_wait("vector", "x1q")
            for h in range(H):
                bi = h % 2
                g_wait("sync", ("wsl", bi))
                tok = s.dma("sync", wsl[bi][:].rearrange("p (kq j) -> p kq j", kq=16),
                            wq_t[h].rearrange("(kq p) j -> p kq j", p=128))
                s.wait("tensor", tok)
                emit_rope_head(qhat, CH * h, wsl[bi])
                guard[("wsl", bi)] = guard["pproj"]
            guard["cosbuf"] = guard["pproj"]
            g_wait("vector", "yhat_w")
            for h in range(H):
                kvh = h // 4
                for m in range(WK):
                    ring = (c + m // 4) % 3
                    sub = m % 4
                    ksl = kr[ring][:, CH * kvh + 128 * sub:CH * kvh + 128 * (sub + 1)]
                    vsl = vr[ring][:, CH * sub + 128 * kvh:CH * sub + 128 * (kvh + 1)]
                    pb = [p_sc0, p_sc1][m % 2]
                    pkey = ("psc0", "psc1")[m % 2]
                    g_wait("tensor", pkey)
                    e_mm(s, pb[:, :], ksl, qhat[:, CH * h:CH * (h + 1)],
                         start=True, stop=True)
                    tsc = s.mark("tensor")
                    s.wait("scalar", tsc)
                    g_wait("scalar", ("P", m % 2))
                    e_act(s, Pt[m % 2][:], pb[:, :], AF.Exp,
                          bias=biasT[:, WK * c + m:WK * c + m + 1])
                    texp = s.mark("scalar")
                    guard[pkey] = texp
                    if m >= 8:
                        s.wait("vector", texp)
                        e_tt(s, "vector", Pt[m % 2][:], Pt[m % 2][:],
                             masks[:, CH * (m - 8):CH * (m - 8 + 1)], ALU.mult)
                        tuse = s.mark("vector")
                    else:
                        tuse = texp
                    if m == 0:
                        g_wait("tensor", "prow")
                        g_wait("tensor", "py")
                    s.wait("tensor", tuse)
                    e_mm(s, p_row[:, :], onesc[:, :], Pt[m % 2][:],
                         start=(m == 0), stop=(m == WK - 1))
                    e_mm(s, p_y[:, :], vsl, Pt[m % 2][:],
                         start=(m == 0), stop=(m == WK - 1))
                    guard[("P", m % 2)] = s.mark("tensor")
                tlast = guard[("P", (WK - 1) % 2)]
                s.wait("vector", tlast)
                g_wait("vector", "srow")
                e_recip(s, srow[:], p_row[:, :])
                trc = s.fence("vector")
                guard["prow"] = trc
                g_wait("tensor", "pbc")
                s.wait("tensor", trc)
                e_mm(s, p_bc[:, :], onesr[:, :], srow[:], start=True, stop=True)
                guard["srow"] = tbc = s.mark("tensor")
                s.wait("vector", tbc)
                e_copy(s, "vector", bcb[:], p_bc[:, :])
                e_tt(s, "vector", yhat[:, CH * h:CH * (h + 1)], p_y[:, :], bcb[:], ALU.mult)
                tyh = s.mark("vector")
                guard["pbc"] = tyh
                guard["py"] = tyh
                if h == H - 1:
                    guard["yhat_ready"] = tyh
                    guard["qhat_free"] = tlast

        def emit_ffn_chunk(c, row0):
            g_wait("vector", "junk")
            wo_last = None
            for o in range(16):
                bi = o % 2
                g_wait("sync", ("wsl", bi))
                tok = s.dma("sync", wsl[bi][:].rearrange("p (kq j) -> p kq j", kq=16),
                            wo_t[o].rearrange("(kq p) j -> p kq j", p=128))
                s.wait("tensor", tok)
                g_wait("tensor", "py")
                if o == 0:
                    s.wait("tensor", guard["yhat_ready"])
                    guard["yhat_ready"] = None
                for yq in range(16):
                    e_mm(s, p_y[:, :], wsl[bi][:, 128 * yq:128 * (yq + 1)],
                         yhat[:, CH * yq:CH * (yq + 1)], start=(yq == 0), stop=(yq == 15))
                guard[("wsl", bi)] = tmm = s.mark("tensor")
                wo_last = tmm
                s.wait("vector", tmm)
                g_wait("vector", ("dacc", o))
                e_copy(s, "vector", dacc[:, CH * o:CH * (o + 1)], p_y[:, :])
                guard["py"] = s.mark("vector")
            guard["yhat_w"] = wo_last
            for E in range(8):
                for mi in range(16):
                    m = 16 * E + mi
                    bi = mi % 2
                    g_wait("sync", ("wsl", bi))
                    g_wait("sync", ("wup", bi))
                    tg = s.dma("sync", wsl[bi][:].rearrange("p (kq j) -> p kq j", kq=16),
                               dg_t[m].rearrange("kq p j -> p kq j"))
                    tu = s.dma("sync", wup[bi][:].rearrange("p (kq j) -> p kq j", kq=16),
                               du_t[m].rearrange("kq p j -> p kq j"))
                    s.waits("tensor", [tg, tu])
                    g_wait("tensor", "psc0")
                    g_wait("tensor", "psc1")
                    for kq in range(16):
                        e_mm(s, p_sc0[:, :], wsl[bi][:, 128 * kq:128 * (kq + 1)],
                             xnfm[:, CH * kq:CH * kq + CH], start=(kq == 0), stop=(kq == 15))
                    guard[("wsl", bi)] = s.mark("tensor")
                    for kq in range(16):
                        e_mm(s, p_sc1[:, :], wup[bi][:, 128 * kq:128 * (kq + 1)],
                             xnfm[:, CH * kq:CH * kq + CH], start=(kq == 0), stop=(kq == 15))
                    guard[("wup", bi)] = tmm = s.mark("tensor")
                    s.wait("vector", tmm)
                    g_wait("vector", "gelu")
                    e_copy(s, "vector", gbuf[:], p_sc0[:, :])
                    tgb = s.fence("vector")
                    guard["psc0"] = tgb
                    s.wait("gpsimd", tgb)
                    e_tt(s, "gpsimd", g2buf[:], gbuf[:], gbuf[:], ALU.mult)
                    e_ts(s, "gpsimd", wbuf[:], g2buf[:], GELU_B, GELU_A, ALU.mult, ALU.add)
                    e_tt(s, "gpsimd", wbuf[:], wbuf[:], gbuf[:], ALU.mult)
                    tgp = s.mark("gpsimd")
                    s.wait("scalar", tgp)
                    e_act(s, sgb[:], wbuf[:], AF.Sigmoid)
                    tsg = s.mark("scalar")
                    s.wait("gpsimd", tsg)
                    e_tt(s, "gpsimd", preb[:], gbuf[:], sgb[:], ALU.mult)
                    tpre = s.mark("gpsimd")
                    s.wait("vector", tpre)
                    g_wait("vector", ("R1", mi // 8))
                    e_tt(s, "vector", it_ap(mi), p_sc1[:, :], preb[:], ALU.mult)
                    guard["gelu"] = tgl = s.mark("vector")
                    guard["psc1"] = tgl
                    guard[("itile", mi)] = tgl
                for hs in range(16):
                    bi = hs % 2
                    g_wait("sync", ("wdn", bi))
                    tok = s.dma("sync", wdn[bi][:].rearrange("p (cc j) -> p cc j", cc=16),
                                dd_t[hs, 16 * E:16 * (E + 1)].rearrange("cc p j -> p cc j"))
                    s.wait("tensor", tok)
                    g_wait("tensor", "py")
                    if hs == 0:
                        for i in range(16):
                            t = guard.get(("itile", i))
                            if t is not None:
                                s.wait("tensor", t)
                                guard[("itile", i)] = None
                    for cc in range(16):
                        e_mm(s, p_y[:, :], wdn[bi][:, 128 * cc:128 * (cc + 1)],
                             it_ap(cc), start=(cc == 0), stop=(cc == 15))
                    guard[("wdn", bi)] = tmm = s.mark("tensor")
                    s.wait("vector", tmm)
                    e_tt(s, "vector", dacc[:, CH * hs:CH * (hs + 1)],
                         dacc[:, CH * hs:CH * (hs + 1)], p_y[:, :], ALU.add)
                    guard["py"] = s.mark("vector")
                guard[("R1", 0)] = tmm
                guard[("R1", 1)] = tmm
            g_wait("vector", "qhat_free")
            last_dacc_reader = None
            for ts4 in range(4):
                bi = ts4 % 2
                g_wait("sync", ("R1", bi))
                tok = s.dma("sync", R1[bi][:],
                            xa[row0 + 128 * ts4:row0 + 128 * (ts4 + 1), :])
                g_wait("vector", "x1q")
                tprev = None
                for o in range(16):
                    if tprev is not None:
                        s.wait("tensor", tprev)
                    elif ts4 == 0:
                        s.wait("tensor", guard["py"])
                    g_wait("tensor", "ptrf")
                    e_tr(s, p_trf[:, :],
                         dacc[:, CH * o + 128 * ts4:CH * o + 128 * (ts4 + 1)], identf[:])
                    ttr = s.mark("tensor")
                    s.wait("vector", ttr)
                    if o == 0:
                        s.wait("vector", tok)
                    e_tt(s, "vector", x1buf[:, 128 * o:128 * (o + 1)], p_trf[:, :],
                         R1[bi][:, 128 * o:128 * (o + 1)], ALU.add)
                    tprev = s.mark("vector")
                    guard["ptrf"] = tprev
                guard[("R1", bi)] = tprev
                last_dacc_reader = tprev
                s.wait("gpsimd", tprev)
                guard["x1q"] = s.dma(
                    "gpsimd",
                    x1[(row0 - HALO) + 128 * ts4:(row0 - HALO) + 128 * (ts4 + 1), :],
                    x1buf[:])
            for o in range(16):
                guard[("dacc", o)] = last_dacc_reader
            guard["junk"] = last_dacc_reader

        for g in range(2):
            emit_ln_chunk(CH * g)
            emit_kv_chunk(g % 3)
        for c in range(NCH):
            row0 = HALO + CH * c
            emit_ln_chunk(row0)
            emit_kv_chunk((c + 2) % 3)
            emit_attn_chunk(c)
            emit_ffn_chunk(c, row0)
            guard["xnfm"] = guard.get(("wup", 1)) or guard.get(("wup", 0))

        s.wait("gpsimd", guard["x1q"])
        s.build_block(stack)
    return nc




def build_b():
    nc = bass.Bass("TRN2", target_bir_lowering=False)
    s = Sched(nc)

    x1 = nc.dram_tensor("x1", [TC, D], F32, kind="ExternalInput").ap()
    Wq_t = nc.dram_tensor("Wq_t", [H, D, DH], BF16, kind="ExternalInput").ap()
    Wkvd_t = nc.dram_tensor("Wkvd_t", [2, D, 128], BF16, kind="ExternalInput").ap()
    kvw_c = nc.dram_tensor("kvw_c", [128, 2], F32, kind="ExternalInput").ap()
    onesf_in = nc.dram_tensor("onesf_in", [128, 1], F32, kind="ExternalInput").ap()
    onesr_in = nc.dram_tensor("onesr_in", [1, 128], F32, kind="ExternalInput").ap()
    identb_in = nc.dram_tensor("identb_in", [128, 128], BF16, kind="ExternalInput").ap()
    cbias_in = nc.dram_tensor("cbias_in", [128, 2], F32, kind="ExternalInput").ap()

    xnfm_o = nc.dram_tensor("xnfm_o", [NCH, 128, 16 * CH], BF16, kind="ExternalOutput").ap()
    lat_o = nc.dram_tensor("lat_o", [2, 128, TC], BF16, kind="ExternalOutput").ap()
    qh_o = nc.dram_tensor("qh_o", [H, 128, TC], BF16, kind="ExternalOutput").ap()
    rho_o = nc.dram_tensor("rho_o", [16, 128], F32, kind="ExternalOutput").ap()

    with ExitStack() as stack:
        sb = lambda n_, sh, d: stack.enter_context(nc.sbuf_tensor(n_, sh, d))
        ps = lambda n_, sh, d: stack.enter_context(nc.psum_tensor(n_, sh, d))
        onesf = sb("onesf", [128, 1], F32)
        onesr = sb("onesr", [1, 128], F32)
        identb = sb("identb", [128, 128], BF16)
        cbias = sb("cbias", [128, 2], F32)
        kvw = sb("kvw", [128, 2], F32)
        xt = [sb(f"xt{i}", [128, D], F32) for i in range(2)]
        xnb = [sb(f"xnb{i}", [128, D], BF16) for i in range(4)]
        xnfm = sb("xnfm", [128, 16 * CH], BF16)
        wsl = [sb(f"wsl{i}", [128, 2048], BF16) for i in range(2)]
        zsq = sb("zsq", [128, CH], F32)
        bcb = sb("bcb", [128, CH], F32)
        srow = sb("srow", [1, CH], F32)
        cols = sb("cols", [128, 8], F32)
        junk = sb("junk", [128, D], F32)
        otile = sb("otile", [128, CH], BF16)
        p_a = ps("p_a", [128, CH], F32)
        p_b = ps("p_b", [128, CH], F32)
        p_row = ps("p_row", [1, CH], F32)
        p_bc = ps("p_bc", [128, CH], F32)
        p_trb = ps("p_trb", [128, 128], BF16)

        toks = [s.dma("sync", onesf[:], onesf_in), s.dma("sync", onesr[:], onesr_in),
                s.dma("sync", identb[:], identb_in), s.dma("sync", cbias[:], cbias_in),
                s.dma("sync", kvw[:], kvw_c)]
        s.waits("vector", toks)
        s.waits("tensor", toks)
        s.waits("scalar", [toks[3]])

        guard = {}

        def g_wait(eng, key):
            t = guard.get(key)
            if t is not None:
                s.wait(eng, t)
                guard[key] = None

        for ct in range(NCH):
            for ts4 in range(4):
                bi = ts4 % 2
                g_wait("sync", ("xt", bi))
                tok = s.dma("sync", xt[bi][:], x1[CH * ct + 128 * ts4:CH * ct + 128 * (ts4 + 1), :])
                s.wait("vector", tok)
                s.wait("scalar", tok)
                xtt = xt[bi]
                s.op("vector", lambda xtt=xtt: nc.vector.reduce_sum(
                    cols[:, 0:1], xtt[:], axis=AX.X))
                s.fence("vector")
                g_wait("scalar", "junk")
                e_act(s, junk[:], xtt[:], AF.Square, accum_out=cols[:, 1:2])
                guard["junk"] = tka = s.mark("scalar")
                e_ts(s, "vector", cols[:, 2:3], cols[:, 0:1], 1.0 / D, None, ALU.mult)
                s.fence("vector")
                e_tt(s, "vector", cols[:, 3:4], cols[:, 2:3], cols[:, 2:3], ALU.mult)
                s.wait("vector", tka)
                s.fence("vector")
                e_stt(s, "vector", cols[:, 4:5], cols[:, 1:2], 1.0 / D, cols[:, 3:4],
                      ALU.mult, ALU.subtract)
                tkv = s.mark("vector")
                s.wait("scalar", tkv)
                e_act(s, cols[:, 5:6], cols[:, 4:5], AF.Sqrt, bias=cbias[:, 0:1])
                tks = s.mark("scalar")
                s.wait("vector", tks)
                e_recip(s, cols[:, 6:7], cols[:, 5:6])
                s.fence("vector")
                e_tt(s, "vector", cols[:, 7:8], cols[:, 6:7], cols[:, 6:7], ALU.mult)
                s.fence("vector")
                e_tt(s, "vector", cols[:, 7:8], cols[:, 7:8], cols[:, 4:5], ALU.mult)
                trh = s.mark("vector")
                s.wait("scalar", trh)
                e_act(s, cols[:, 7:8], cols[:, 7:8], AF.Sqrt, bias=cbias[:, 0:1])
                trs = s.mark("scalar")
                s.wait("vector", trs)
                e_recip(s, cols[:, 7:8], cols[:, 7:8])
                trr = s.fence("vector")
                s.wait("gpsimd", trr)
                s.dma("gpsimd",
                      rho_o[4 * ct + ts4:4 * ct + ts4 + 1, :].rearrange("one p -> p one"),
                      cols[:, 7:8])
                g_wait("vector", ("xnb", ts4))
                for rho_i in range(2):
                    for par in range(2):
                        out_ap = bass.AP(xnb[ts4][:].tensor, 128 * par + 64 * rho_i,
                                         [[2048, 128], [1, 64], [256, 8]])
                        in_ap = bass.AP(xtt[:].tensor, 2 * rho_i + par,
                                        [[2048, 128], [32, 64], [4, 8]])

                        def fn(out_ap=out_ap, in_ap=in_ap):
                            return nc.vector.tensor_scalar(
                                out=out_ap, in0=in_ap, scalar1=cols[:, 2:3],
                                scalar2=cols[:, 6:7], op0=ALU.subtract, op1=ALU.mult)
                        s.op("vector", fn)
                guard[("xt", bi)] = tfin = s.mark("vector")
                s.wait("tensor", tfin)
            g_wait("vector", "xnfm")
            g_wait("vector", "xnfm_dma")
            g_wait("tensor", "ptrb")
            tr_tok = None
            for kq in range(16):
                for ts4 in range(4):
                    if tr_tok is not None:
                        s.wait("tensor", tr_tok)
                    e_tr(s, p_trb[:, :], xnb[ts4][:, 128 * kq:128 * (kq + 1)], identb[:])
                    tk = s.mark("tensor")
                    s.wait("vector", tk)
                    e_copy(s, "vector",
                           xnfm[:, CH * kq + 128 * ts4:CH * kq + 128 * (ts4 + 1)],
                           p_trb[:, :])
                    tr_tok = s.mark("vector")
            for i in range(4):
                guard[("xnb", i)] = tr_tok
            guard["ptrb"] = tr_tok
            s.wait("tensor", tr_tok)
            s.wait("gpsimd", tr_tok)
            guard["xnfm_dma"] = s.dma("gpsimd", xnfm_o[ct], xnfm[:])

            banks = [p_a, p_b]
            g_wait("tensor", "pa")
            g_wait("tensor", "pb")
            for rc in range(2):
                bi = rc
                g_wait("sync", ("wsl", bi))
                tok = s.dma("sync", wsl[bi][:].rearrange("p (kq j) -> p kq j", kq=16),
                            Wkvd_t[rc].rearrange("(kq p) j -> p kq j", p=128))
                s.wait("tensor", tok)
                for kq in range(16):
                    e_mm(s, banks[rc][:, :], wsl[bi][:, 128 * kq:128 * (kq + 1)],
                         xnfm[:, CH * kq:CH * kq + CH], start=(kq == 0), stop=(kq == 15))
                guard[("wsl", bi)] = s.mark("tensor")
            tmm = guard[("wsl", 1)]
            s.wait("scalar", tmm)
            g_wait("scalar", "zsq")
            e_act(s, zsq[:], p_a[:, :], AF.Square)
            tz0 = s.mark("scalar")
            g_wait("tensor", "prow")
            s.wait("tensor", tz0)
            e_mm(s, p_row[:, :], onesf[:, :], zsq[:], start=True, stop=False)
            s.mark("tensor")
            s.wait("scalar", s.mark("tensor"))
            e_act(s, zsq[:], p_b[:, :], AF.Square)
            tz1 = s.mark("scalar")
            s.wait("tensor", tz1)
            e_mm(s, p_row[:, :], onesf[:, :], zsq[:], start=False, stop=True)
            guard["zsq"] = trow = s.mark("tensor")
            s.wait("scalar", trow)
            e_act(s, srow[:], p_row[:, :], AF.Sqrt, bias=cbias[0:1, 1:2], scale=1.0 / 256)
            guard["prow"] = tsr = s.mark("scalar")
            s.wait("vector", tsr)
            e_recip(s, srow[:], srow[:])
            trr = s.fence("vector")
            g_wait("tensor", "pbc")
            s.wait("tensor", trr)
            e_mm(s, p_bc[:, :], onesr[:, :], srow[:], start=True, stop=True)
            tbc = s.mark("tensor")
            s.wait("vector", tbc)
            e_copy(s, "vector", bcb[:], p_bc[:, :])
            s.fence("vector")
            for rc in range(2):
                g_wait("vector", ("ot", rc))
                e_stt(s, "vector", otile[:], banks[rc][:, :], kvw[:, rc:rc + 1],
                      bcb[:], ALU.mult, ALU.mult)
                tot = s.mark("vector")
                guard["pbc" if rc == 1 else f"_pbc{rc}"] = tot
                s.wait("gpsimd", tot)
                guard[("ot", rc)] = s.dma("gpsimd", lat_o[rc, :, CH * ct:CH * (ct + 1)],
                                          otile[:])
                g_wait("vector", ("ot", rc))
            guard["pa"] = guard["pb"] = s.mark("vector")

            for h in range(H):
                bi = h % 2
                g_wait("sync", ("wsl", bi))
                tok = s.dma("sync", wsl[bi][:].rearrange("p (kq j) -> p kq j", kq=16),
                            Wq_t[h].rearrange("(kq p) j -> p kq j", p=128))
                s.wait("tensor", tok)
                g_wait("tensor", "pa")
                for kq in range(16):
                    e_mm(s, p_a[:, :], wsl[bi][:, 128 * kq:128 * (kq + 1)],
                         xnfm[:, CH * kq:CH * kq + CH], start=(kq == 0), stop=(kq == 15))
                guard[("wsl", bi)] = tmm = s.mark("tensor")
                s.wait("vector", tmm)
                g_wait("vector", ("qo", h % 2))
                e_copy(s, "vector", otile[:], p_a[:, :])
                guard["pa"] = tqc = s.mark("vector")
                s.wait("gpsimd", tqc)
                guard[("qo", h % 2)] = s.dma("gpsimd", qh_o[h, :, CH * ct:CH * (ct + 1)],
                                             otile[:])
                g_wait("vector", ("qo", h % 2))
            guard["xnfm"] = guard[("wsl", (H - 1) % 2)]

        s.wait("gpsimd", ("dma", "gpsimd", 16 * s.ndma["gpsimd"]))
        s.build_block(stack)
    return nc



def build_c():
    nc = bass.Bass("TRN2", target_bir_lowering=False)
    s = Sched(nc)

    q4 = nc.dram_tensor("q4", [4, 128, T], BF16, kind="ExternalInput").ap()
    latf = nc.dram_tensor("latf", [2, 128, T], BF16, kind="ExternalInput").ap()
    Wkvu_t = nc.dram_tensor("Wkvu_t", [2, 128, 4096], BF16, kind="ExternalInput").ap()
    masks_in = nc.dram_tensor("masks_in", [128, 4 * CH], BF16, kind="ExternalInput").ap()
    onesc_in = nc.dram_tensor("onesc_in", [128, 1], BF16, kind="ExternalInput").ap()
    onesr_in = nc.dram_tensor("onesr_in", [1, 128], F32, kind="ExternalInput").ap()
    hsel = nc.dram_tensor("hsel", [1, 4], I32, kind="ExternalInput").ap()

    y4 = nc.dram_tensor("y4", [4, 128, T], BF16, kind="ExternalOutput").ap()

    NKC = T // CH
    with ExitStack() as stack:
        sb = lambda n_, sh, d: stack.enter_context(nc.sbuf_tensor(n_, sh, d))
        ps = lambda n_, sh, d: stack.enter_context(nc.psum_tensor(n_, sh, d))
        masks = sb("masks", [128, 4 * CH], BF16)
        onesc = sb("onesc", [128, 1], BF16)
        onesr = sb("onesr", [1, 128], F32)
        lat = sb("lat", [128, 2 * T], BF16)
        wkv = sb("wkv", [128, 2 * 4096], BF16)
        qp = sb("qp", [128, T], BF16)
        kfm = sb("kfm", [128, T], BF16)
        vT = sb("vT", [128, T], BF16)
        Pt = [sb(f"P{i}", [128, CH], BF16) for i in range(2)]
        bcb = sb("bcb", [128, CH], F32)
        srow = sb("srow", [1, CH], F32)
        ytile = sb("ytile", [128, CH], BF16)
        p_sc0 = ps("p_sc0", [128, CH], F32)
        p_sc1 = ps("p_sc1", [128, CH], F32)
        p_y = ps("p_y", [128, CH], F32)
        p_row = ps("p_row", [1, CH], F32)
        p_bc = ps("p_bc", [128, CH], F32)
        p_k = ps("p_k", [128, CH], F32)
        p_v = ps("p_v", [128, 128], F32)

        toks = [s.dma("sync", masks[:], masks_in), s.dma("sync", onesc[:], onesc_in),
                s.dma("sync", onesr[:], onesr_in),
                s.dma("sync", lat[:].rearrange("p (rc t) -> p rc t", rc=2), latf),
                s.dma("sync", wkv[:].rearrange("p (rc j) -> p rc j", rc=2), Wkvu_t)]
        s.waits("tensor", toks)
        s.waits("vector", toks)
        s.waits("scalar", toks)

        guard = {}

        def g_wait(eng, key):
            t = guard.get(key)
            if t is not None:
                s.wait(eng, t)
                guard[key] = None

        for p4 in range(4):
            tq = s.dma("sync", qp[:], q4[p4])
            s.wait("tensor", tq)
            kcol = 1024 * p4
            vcol = 1024 * p4 + 512
            g_wait("tensor", "kv_built")
            for kc in range(NKC):
                g_wait("tensor", "pk")
                for rc in range(2):
                    e_mm(s, p_k[:, :], wkv[:, 4096 * rc + kcol:4096 * rc + kcol + 128],
                         lat[:, T * rc + CH * kc:T * rc + CH * kc + CH],
                         start=(rc == 0), stop=(rc == 1))
                tk = s.mark("tensor")
                s.wait("vector", tk)
                g_wait("vector", ("kfm_w", 0))
                e_copy(s, "vector", kfm[:, CH * kc:CH * (kc + 1)], p_k[:, :])
                guard["pk"] = guard[("kfm_w", 0)] = s.mark("vector")
                for sub in range(4):
                    kk = 4 * kc + sub
                    g_wait("tensor", "pv")
                    for rc in range(2):
                        e_mm(s, p_v[:, :],
                             lat[:, T * rc + 128 * kk:T * rc + 128 * (kk + 1)],
                             wkv[:, 4096 * rc + vcol:4096 * rc + vcol + 128],
                             start=(rc == 0), stop=(rc == 1))
                    tv = s.mark("tensor")
                    s.wait("vector", tv)
                    e_copy(s, "vector", vT[:, 128 * kk:128 * (kk + 1)], p_v[:, :])
                    guard["pv"] = s.mark("vector")
            s.wait("tensor", guard["pv"])
            s.wait("tensor", guard["pk"]) if guard.get("pk") else None
            guard["pv"] = None
            guard["pk"] = None
            last_sc = {0: None, 1: None}
            for qt in range(NKC):
                nkk = 4 * (qt + 1)
                g_wait("tensor", "prow")
                g_wait("tensor", "py")
                for kk in range(nkk):
                    pb = [p_sc0, p_sc1][kk % 2]
                    if last_sc[kk % 2] is not None:
                        s.wait("tensor", last_sc[kk % 2])
                        last_sc[kk % 2] = None
                    e_mm(s, pb[:, :], kfm[:, 128 * kk:128 * (kk + 1)],
                         qp[:, CH * qt:CH * (qt + 1)], start=True, stop=True)
                    tsc = s.mark("tensor")
                    s.wait("scalar", tsc)
                    g_wait("scalar", ("P", kk % 2))
                    e_act(s, Pt[kk % 2][:], pb[:, :], AF.Exp, scale=SCALE_QK)
                    texp = s.mark("scalar")
                    last_sc[kk % 2] = texp
                    if kk >= nkk - 4:
                        s.wait("vector", texp)
                        e_tt(s, "vector", Pt[kk % 2][:], Pt[kk % 2][:],
                             masks[:, CH * (kk - (nkk - 4)):CH * (kk - (nkk - 4) + 1)],
                             ALU.mult)
                        tuse = s.mark("vector")
                    else:
                        tuse = texp
                    s.wait("tensor", tuse)
                    e_mm(s, p_row[:, :], onesc[:, :], Pt[kk % 2][:],
                         start=(kk == 0), stop=(kk == nkk - 1))
                    e_mm(s, p_y[:, :], vT[:, 128 * kk:128 * (kk + 1)], Pt[kk % 2][:],
                         start=(kk == 0), stop=(kk == nkk - 1))
                    guard[("P", kk % 2)] = s.mark("tensor")
                tl = guard[("P", (nkk - 1) % 2)]
                s.wait("vector", tl)
                g_wait("vector", "srow")
                e_recip(s, srow[:], p_row[:, :])
                trc = s.fence("vector")
                guard["prow"] = trc
                g_wait("tensor", "pbc")
                s.wait("tensor", trc)
                e_mm(s, p_bc[:, :], onesr[:, :], srow[:], start=True, stop=True)
                guard["srow"] = tbc = s.mark("tensor")
                s.wait("vector", tbc)
                e_copy(s, "vector", bcb[:], p_bc[:, :])
                g_wait("vector", "yt")
                e_tt(s, "vector", ytile[:], p_y[:, :], bcb[:], ALU.mult)
                tyt = s.mark("vector")
                guard["pbc"] = tyt
                guard["py"] = tyt
                s.wait("gpsimd", tyt)
                guard["yt"] = s.dma("gpsimd", y4[p4, :, CH * qt:CH * (qt + 1)], ytile[:])
            guard["kv_built"] = guard[("P", 0)] or guard[("P", 1)]
        s.wait("gpsimd", ("dma", "gpsimd", 16 * s.ndma["gpsimd"]))
        s.build_block(stack)
    return nc



def build_d():
    nc = bass.Bass("TRN2", target_bir_lowering=False)
    s = Sched(nc)

    x1 = nc.dram_tensor("x1", [TC, D], F32, kind="ExternalInput").ap()
    yall = nc.dram_tensor("yall", [D, TC], BF16, kind="ExternalInput").ap()
    xnfm_i = nc.dram_tensor("xnfm_i", [NCH, 128, 16 * CH], BF16, kind="ExternalInput").ap()
    rho4 = nc.dram_tensor("rho4", [NCH, CH], F32, kind="ExternalInput").ap()
    Wo_t = nc.dram_tensor("Wo_t", [16, D, 128], BF16, kind="ExternalInput").ap()
    sw_t = nc.dram_tensor("sw_t", [16, D, 128], BF16, kind="ExternalInput").ap()
    lnw_c = nc.dram_tensor("lnw_c", [128, 16], F32, kind="ExternalInput").ap()
    lnb_c = nc.dram_tensor("lnb_c", [128, 16], BF16, kind="ExternalInput").ap()
    gam_c = nc.dram_tensor("gam_c", [128, 16], F32, kind="ExternalInput").ap()
    bet_c = nc.dram_tensor("bet_c", [128, 16], F32, kind="ExternalInput").ap()
    sc8 = nc.dram_tensor("sc8", [1, 8], F32, kind="ExternalInput").ap()
    onesr_in = nc.dram_tensor("onesr_in", [1, 128], F32, kind="ExternalInput").ap()
    identf_in = nc.dram_tensor("identf_in", [128, 128], F32, kind="ExternalInput").ap()

    out = nc.dram_tensor("out", [TC, D], F32, kind="ExternalOutput").ap()

    with ExitStack() as stack:
        sb = lambda n_, sh, d: stack.enter_context(nc.sbuf_tensor(n_, sh, d))
        ps = lambda n_, sh, d: stack.enter_context(nc.psum_tensor(n_, sh, d))
        onesr = sb("onesr", [1, 128], F32)
        identf = sb("identf", [128, 128], F32)
        lnw = sb("lnw", [128, 16], F32)
        lnb = sb("lnb", [128, 16], BF16)
        gam = sb("gam", [128, 16], F32)
        bet = sb("bet", [128, 16], F32)
        s8 = sb("s8", [1, 8], F32)
        s1 = sb("s1", [1, 1], F32)
        scol = sb("scol", [128, 1], F32)
        sg = sb("sg", [128, 16], F32)
        bv = sb("bv", [128, 16], F32)
        cb = sb("cb", [128, 16], F32)
        xnfm = sb("xnfm", [128, 16 * CH], BF16)
        xnw = sb("xnw", [128, 16 * CH], BF16)
        yt = sb("yt", [128, 16 * CH], BF16)
        rhob = sb("rhob", [128, CH], F32)
        srow = sb("srow", [1, CH], F32)
        wsl = [sb(f"wsl{i}", [128, 2048], BF16) for i in range(2)]
        ssl = [sb(f"ssl{i}", [128, 2048], BF16) for i in range(2)]
        dacc = sb("dacc", [128, 16 * CH], F32)
        t1b = sb("t1b", [128, CH], F32)
        xrt = [sb(f"xrt{i}", [128, D], F32) for i in range(2)]
        x1b = sb("x1b", [128, D], F32)
        p_a = ps("p_a", [128, CH], F32)
        p_b = ps("p_b", [128, CH], F32)
        p_bc = ps("p_bc", [128, CH], F32)
        p_c = ps("p_c", [128, 1], F32)
        p_trf = ps("p_trf", [128, 128], F32)

        toks = [s.dma("sync", onesr[:], onesr_in), s.dma("sync", identf[:], identf_in),
                s.dma("sync", lnw[:], lnw_c), s.dma("sync", lnb[:], lnb_c),
                s.dma("sync", gam[:], gam_c), s.dma("sync", bet[:], bet_c),
                s.dma("sync", s8[:], sc8)]
        s.waits("vector", toks)
        s.waits("tensor", toks)

        guard = {}

        def g_wait(eng, key):
            t = guard.get(key)
            if t is not None:
                s.wait(eng, t)
                guard[key] = None

        e_red(s, s1[:], s8[:], ALU.max)
        tv = s.fence("vector")
        s.wait("tensor", tv)
        e_mm(s, p_bc[0:128, 0:1], onesr[:, :], s1[:, :], start=True, stop=True)
        tmm = s.mark("tensor")
        s.wait("vector", tmm)
        e_copy(s, "vector", scol[:], p_bc[:, 0:1])
        s.fence("vector")
        e_ts(s, "vector", sg[:], gam[:], scol[:, 0:1], None, ALU.mult)
        tsg_ = s.mark("vector")
        s.wait("tensor", tsg_)
        for o in range(16):
            bi = o % 2
            g_wait("sync", ("ssl", bi))
            tok = s.dma("sync", ssl[bi][:].rearrange("p (kq j) -> p kq j", kq=16),
                        sw_t[o].rearrange("(kq p) j -> p kq j", p=128))
            s.wait("tensor", tok)
            g_wait("tensor", "pc")
            for kq in range(16):
                e_mm(s, p_c[:, :], ssl[bi][:, 128 * kq:128 * (kq + 1)],
                     lnb[:, kq:kq + 1], start=(kq == 0), stop=(kq == 15))
            guard[("ssl", bi)] = tb = s.mark("tensor")
            s.wait("vector", tb)
            e_copy(s, "vector", bv[:, o:o + 1], p_c[:, :])
            guard["pc"] = s.mark("vector")
        s.fence("vector")
        e_tt(s, "vector", cb[:], bv[:], sg[:], ALU.mult)
        s.fence("vector")
        e_tt(s, "vector", cb[:], cb[:], bet[:], ALU.add)
        s.fence("vector")

        for ct in range(NCH):
            tx = s.dma("sync", xnfm[:], xnfm_i[ct])
            s.wait("vector", tx)
            g_wait("vector", "xnw")
            for kq in range(16):
                e_ts(s, "vector", xnw[:, CH * kq:CH * (kq + 1)],
                     xnfm[:, CH * kq:CH * (kq + 1)], lnw[:, kq:kq + 1], None, ALU.mult)
            txw = s.mark("vector")
            s.wait("tensor", txw)
            ty = s.dma("sync", yt[:].rearrange("p (kq j) -> p kq j", kq=16),
                       yall.rearrange("(kq p) j -> p kq j", p=128)[:, :, CH * ct:CH * (ct + 1)])
            s.wait("tensor", ty)
            tr_ = s.dma("sync", srow[:], rho4[ct:ct + 1, :])
            s.wait("tensor", tr_)
            g_wait("tensor", "pbc")
            e_mm(s, p_bc[:, :], onesr[:, :], srow[:], start=True, stop=True)
            trb = s.mark("tensor")
            s.wait("vector", trb)
            e_copy(s, "vector", rhob[:], p_bc[:, :])
            trv = s.mark("vector")
            guard["pbc"] = trv
            for o in range(16):
                bi = o % 2
                g_wait("sync", ("wsl", bi))
                t1 = s.dma("sync", wsl[bi][:].rearrange("p (kq j) -> p kq j", kq=16),
                           Wo_t[o].rearrange("(kq p) j -> p kq j", p=128))
                g_wait("sync", ("ssl", bi))
                t2 = s.dma("sync", ssl[bi][:].rearrange("p (kq j) -> p kq j", kq=16),
                           sw_t[o].rearrange("(kq p) j -> p kq j", p=128))
                s.waits("tensor", [t1, t2])
                g_wait("tensor", "pa")
                for kq in range(16):
                    e_mm(s, p_a[:, :], wsl[bi][:, 128 * kq:128 * (kq + 1)],
                         yt[:, CH * kq:CH * (kq + 1)], start=(kq == 0), stop=(kq == 15))
                guard[("wsl", bi)] = s.mark("tensor")
                g_wait("tensor", "pb")
                for kq in range(16):
                    e_mm(s, p_b[:, :], ssl[bi][:, 128 * kq:128 * (kq + 1)],
                         xnw[:, CH * kq:CH * (kq + 1)], start=(kq == 0), stop=(kq == 15))
                guard[("ssl", bi)] = tmm2 = s.mark("tensor")
                s.wait("vector", tmm2)
                g_wait("vector", "t1b")
                e_tt(s, "vector", t1b[:], p_b[:, :], rhob[:], ALU.mult)
                guard["pb"] = s.fence("vector")
                e_ts(s, "vector", t1b[:], t1b[:], sg[:, o:o + 1], cb[:, o:o + 1],
                     ALU.mult, ALU.add)
                s.fence("vector")
                g_wait("vector", ("dacc", o))
                e_tt(s, "vector", dacc[:, CH * o:CH * (o + 1)], t1b[:], p_a[:, :], ALU.add)
                guard["pa"] = guard["t1b"] = s.mark("vector")
            for ts4 in range(4):
                bi = ts4 % 2
                g_wait("sync", ("xrt", bi))
                tok = s.dma("sync", xrt[bi][:],
                            x1[CH * ct + 128 * ts4:CH * ct + 128 * (ts4 + 1), :])
                g_wait("vector", "x1b")
                tprev = None
                for o in range(16):
                    if tprev is not None:
                        s.wait("tensor", tprev)
                    elif ts4 == 0:
                        s.wait("tensor", guard["pa"])
                    g_wait("tensor", "ptrf")
                    e_tr(s, p_trf[:, :],
                         dacc[:, CH * o + 128 * ts4:CH * o + 128 * (ts4 + 1)], identf[:])
                    ttr = s.mark("tensor")
                    s.wait("vector", ttr)
                    if o == 0:
                        s.wait("vector", tok)
                    e_tt(s, "vector", x1b[:, 128 * o:128 * (o + 1)], p_trf[:, :],
                         xrt[bi][:, 128 * o:128 * (o + 1)], ALU.add)
                    tprev = s.mark("vector")
                    guard["ptrf"] = tprev
                guard[("xrt", bi)] = tprev
                s.wait("gpsimd", tprev)
                guard["x1b"] = s.dma("gpsimd",
                                     out[CH * ct + 128 * ts4:CH * ct + 128 * (ts4 + 1), :],
                                     x1b[:])
            for o in range(16):
                guard[("dacc", o)] = guard["x1b"] and tprev
        s.wait("gpsimd", ("dma", "gpsimd", 16 * s.ndma["gpsimd"]))
        s.build_block(stack)
    return nc


def _rope_tables(q0, n=HALO + TC):
    pos = np.arange(q0 - HALO, q0 - HALO + n, dtype=np.float64)
    pos = np.maximum(pos, 0.0)
    inv = 10000.0 ** (-np.arange(64, dtype=np.float64) / 64.0)
    fr = pos[None, :] * inv[:, None]
    return np.cos(fr).astype(np.float32), np.sin(fr).astype(np.float32)


def _a_consts():
    masks = np.zeros((128, 4 * CH), BF)
    j = np.arange(128)[:, None]
    i = np.arange(CH)[None, :]
    for mo in range(4):
        masks[:, CH * mo:CH * (mo + 1)] = (j + 128 * mo <= i).astype(BF)
    onesc = np.ones((128, 1), BF)
    onesf = np.ones((128, 1), np.float32)
    onesr = np.ones((1, 128), np.float32)
    c297 = np.full((1, 128), RSQRT4_DH, np.float32)
    identb = np.eye(128, dtype=np.float32).astype(BF)
    identf = np.eye(128, dtype=np.float32)
    cbias = np.zeros((128, 2), np.float32)
    cbias[:, 0] = 1e-5
    cbias[:, 1] = 1e-6
    return dict(masks_in=masks, onesc_in=onesc, onesf_in=onesf, onesr_in=onesr,
                c297_in=c297, identb_in=identb, identf_in=identf, cbias_in=cbias)


def _a_inputs(inputs, zout):
    x = np.asarray(inputs["L0_attn_x"])
    wq_t = np.ascontiguousarray(zout["wq"].reshape(D, H, DH).transpose(1, 0, 2))
    wk_t = np.ascontiguousarray(zout["wk"].reshape(D, KVH, DH).transpose(1, 0, 2))
    wv_t = np.ascontiguousarray(zout["wv"].reshape(16, 128, 512))
    wo_t = np.ascontiguousarray(zout["wo"].reshape(D, 16, 128).transpose(1, 0, 2))
    dg_t = np.ascontiguousarray(zout["dG"].reshape(16, 128, 128, 128).transpose(2, 0, 1, 3))
    du_t = np.ascontiguousarray(zout["dU"].reshape(16, 128, 128, 128).transpose(2, 0, 1, 3))
    dd_t = np.ascontiguousarray(zout["dD"].reshape(128, 128, 16, 128).transpose(2, 0, 1, 3))
    consts = _a_consts()
    in_maps = []
    for c in range(NC):
        b, q0 = c // 4, (c % 4) * TC
        xa = np.zeros((HALO + TC, D), np.float32)
        if q0 > 0:
            xa[:HALO] = x[b, q0 - HALO:q0]
        xa[HALO:] = x[b, q0:q0 + TC]
        cos, sin = _rope_tables(q0)
        biasT = np.zeros((128, NCH * WK), np.float32)
        for ct in range(NCH):
            for m in range(WK):
                k0 = q0 + CH * ct - HALO + 128 * m
                biasT[:, WK * ct + m] = L0_BIAS if k0 >= 0 else L0_BIAS_INVALID
        im = dict(xa=xa, wq_t=wq_t, wk_t=wk_t, wv_t=wv_t, wo_t=wo_t,
                  dg_t=dg_t, du_t=du_t, dd_t=dd_t,
                  cos_in=cos, sin_in=sin, biasT_in=biasT, **consts)
        in_maps.append(im)
    return in_maps


def _b_inputs(x1s, zout):
    Wq_t = np.ascontiguousarray(zout["Wq"].reshape(D, H, DH).transpose(1, 0, 2))
    Wkvd_t = np.ascontiguousarray(zout["Wkvd"].reshape(D, 2, 128).transpose(1, 0, 2))
    c = _a_consts()
    return [dict(x1=x1s[ci], Wq_t=Wq_t, Wkvd_t=Wkvd_t,
                 kvw_c=None,
                 onesf_in=c["onesf_in"], onesr_in=c["onesr_in"],
                 identb_in=c["identb_in"], cbias_in=c["cbias_in"])
            for ci in range(NC)]


def _c_inputs(bouts, zout):
    c = _a_consts()
    Wkvu = zout["Wkvu"]
    in_maps = []
    for ci in range(NC):
        b = ci // 4
        cores_b = [4 * b + j for j in range(4)]
        q4 = np.zeros((4, 128, T), BF)
        wkv = np.zeros((2, 128, 4096), BF)
        for p4 in range(4):
            h = 4 * (ci % 4) + p4
            q4[p4] = np.concatenate([bouts[cp]["qh_o"][h] for cp in cores_b], axis=1)
            wkv[:, :, 1024 * p4:1024 * p4 + 128] = \
                Wkvu[:, DH * h:DH * (h + 1)].reshape(2, 128, 128)
            wkv[:, :, 1024 * p4 + 512:1024 * p4 + 640] = \
                Wkvu[:, D + DH * h:D + DH * (h + 1)].reshape(2, 128, 128)
        latf = np.concatenate([bouts[cp]["lat_o"] for cp in cores_b], axis=2)
        in_maps.append(dict(q4=q4, latf=latf, Wkvu_t=np.ascontiguousarray(wkv),
                            masks_in=c["masks_in"], onesc_in=c["onesc_in"],
                            onesr_in=c["onesr_in"],
                            hsel=np.zeros((1, 4), np.int32)))
    return in_maps


def _d_inputs(x1s, bouts, couts, inputs, zout):
    c = _a_consts()
    Wo_t = np.ascontiguousarray(zout["Wo"].reshape(D, 16, 128).transpose(1, 0, 2))
    sw_t = np.ascontiguousarray(zout["signW"].reshape(D, 16, 128).transpose(1, 0, 2))
    lnw_c = np.ascontiguousarray(
        np.asarray(inputs["L1_ffn_ln_weight"])[PI].reshape(16, 128).T.astype(np.float32))
    lnb_c = np.ascontiguousarray(zout["lnb"].reshape(16, 128).T)
    gam_c = np.ascontiguousarray(
        np.asarray(inputs["L1_ffn_gamma"]).reshape(16, 128).T.astype(np.float32))
    bet_c = np.ascontiguousarray(
        np.asarray(inputs["L1_ffn_beta"]).reshape(16, 128).T.astype(np.float32))
    sc8 = zout["wmax8"].astype(np.float32)
    in_maps = []
    for ci in range(NC):
        b, q0 = ci // 4, (ci % 4) * TC
        yall = np.zeros((D, TC), BF)
        for h in range(H):
            src_core = 4 * b + h // 4
            yall[128 * h:128 * (h + 1)] = couts[src_core]["y4"][h % 4][:, q0:q0 + TC]
        in_maps.append(dict(
            x1=x1s[ci], yall=yall, xnfm_i=bouts[ci]["xnfm_o"],
            rho4=bouts[ci]["rho_o"].reshape(NCH, CH),
            Wo_t=Wo_t, sw_t=sw_t, lnw_c=lnw_c, lnb_c=lnb_c,
            gam_c=gam_c, bet_c=bet_c, sc8=sc8,
            onesr_in=c["onesr_in"], identf_in=c["identf_in"]))
    return in_maps


_NCS = {}


def _get_nc(name, builder):
    if name not in _NCS:
        _NCS[name] = builder()
    return _NCS[name]


def _bf_pack(r):
    return {k: (np.asarray(v).view(np.uint16).view(BF)
                if np.asarray(v).dtype == np.uint16 else np.asarray(v))
            for k, v in r.items()}


def kernel(**inputs):
    inputs = {k: np.asarray(v) for k, v in inputs.items()}
    zin, shapes = _z_inputs(inputs)
    rz = run_bass_kernel_spmd(_get_nc("z", build_z), zin, core_ids=list(range(NC)))
    zout = _z_post(rz.results, shapes)
    ain = _a_inputs(inputs, zout)
    ra = run_bass_kernel_spmd(_get_nc("a", build_a), ain, core_ids=list(range(NC)))
    x1s = [ra.results[c]["x1"] for c in range(NC)]
    bin_ = _b_inputs(x1s, zout)
    kvw_c = np.ascontiguousarray(
        np.asarray(inputs["L1_attn_kv_norm_w"]).reshape(2, 128).T.astype(np.float32))
    for im in bin_:
        im["kvw_c"] = kvw_c
    rb = run_bass_kernel_spmd(_get_nc("b", build_b), bin_, core_ids=list(range(NC)))
    bouts = [rb.results[c] for c in range(NC)]
    cin = _c_inputs(bouts, zout)
    rc_ = run_bass_kernel_spmd(_get_nc("c", build_c), cin, core_ids=list(range(NC)))
    couts = [rc_.results[c] for c in range(NC)]
    din = _d_inputs(x1s, bouts, couts, inputs, zout)
    rd = run_bass_kernel_spmd(_get_nc("d", build_d), din, core_ids=list(range(NC)))
    out = np.empty((B, T, D), np.float32)
    for ci in range(NC):
        b, q0 = ci // 4, (ci % 4) * TC
        out[b, q0:q0 + TC] = rd.results[ci]["out"]
    return out
